# version 01 (final)
# baseline (speedup 1.0000x reference)
import numpy as np
import ml_dtypes
import math
from contextlib import ExitStack

import concourse.bass as bass
import concourse.mybir as mybir
from concourse.bass_utils import run_bass_kernel_spmd

F32 = mybir.dt.float32
BF16 = mybir.dt.bfloat16
AF = mybir.ActivationFunctionType
ALU = mybir.AluOpType
AX = mybir.AxisListType

D = 1024
T = 2048
NCH = D // 128
TT = 512
NTT = T // TT
LN_EPS = 1e-5

ENGS = ("tensor", "vector", "scalar", "gpsimd", "sync")

ALL_BLOCKS = ("l0attn", "l0ffn", "l1attn", "l1ffn")
GROUPS = [ALL_BLOCKS]



class Pl:
    def __init__(self, nc):
        self.nc = nc
        self.q = {e: [] for e in ENGS}
        self.sems = {}
        self.cnt = {}
        self.wm = {e: {} for e in ENGS}
        for s in ("pe", "dve", "act", "pool"):
            self.sems[s] = nc.alloc_semaphore(name="sem_" + s)
            self.cnt[s] = 0
        self.dq = {}
        for q, r in (("qw", 10), ("qx", 6)):
            names = []
            for i in range(r):
                s = f"{q}{i}"
                self.sems[s] = nc.alloc_semaphore(name="sem_" + s)
                self.cnt[s] = 0
                names.append(s)
            self.dq[q] = {"names": names, "idx": 0}

    def emit(self, eng, fn, inc=None, n=1):
        if inc is not None:
            self.cnt[inc] += n
            val = self.cnt[inc]
            sem = self.sems[inc]
            self.q[eng].append(lambda e, fn=fn, sem=sem, n=n: fn(e).then_inc(sem, n))
            return (inc, val)
        self.q[eng].append(fn)
        return None

    def wait(self, eng, toks):
        for t in toks:
            if t is None:
                continue
            s, v = t
            if self.wm[eng].get(s, 0) >= v:
                continue
            self.wm[eng][s] = v
            sem = self.sems[s]
            self.q[eng].append(lambda e, sem=sem, v=v: e.wait_ge(sem, v))

    def set_pads(self, vpad, spad):
        self.pads = {"vector": vpad, "scalar": spad}

    def _pause_fn(self, eng):
        if eng == "vector":
            p = self.pads["vector"]
            return lambda e: e.tensor_copy(p, p)
        if eng == "scalar":
            p = self.pads["scalar"]
            return lambda e: e.copy(p, p)
        return lambda e: e.nop()

    def settle(self, eng, inc, cycles=None):
        return self.emit(eng, self._pause_fn(eng), inc=inc)

    def pause(self, eng, cycles=None):
        self.q[eng].append(self._pause_fn(eng))

    def dma(self, eng, out, in_, queue, **kw):
        ring = self.dq[queue]
        s = ring["names"][ring["idx"]]
        ring["idx"] = (ring["idx"] + 1) % len(ring["names"])
        if self.cnt[s]:
            self.wait(eng, [(s, self.cnt[s])])
        return self.emit(eng, lambda e, o=out, i=in_, kw=kw: e.dma_start(out=o, in_=i, **kw),
                         inc=s, n=16)

    def queue_flush_tokens(self, queue):
        return [(s, self.cnt[s]) for s in self.dq[queue]["names"] if self.cnt[s]]

    def phase_barrier(self):
        toks = [(s, self.cnt[s]) for s in ("pe", "dve", "act", "pool") if self.cnt[s]]
        self.wait("gpsimd", toks)
        self.wait("sync", toks)

    def finalize(self):
        with self.nc.Block() as blk:
            for eng in ENGS:
                ops = self.q[eng]
                if not ops:
                    continue

                def runner(e, ops=ops):
                    for f in ops:
                        f(e)
                getattr(blk, eng)(runner)


class BankPool:

    def __init__(self, nc, n=6):
        self.banks = [nc.alloc_psum_tensor(f"pb{i}", [128, 512], F32) for i in range(n)]
        self.tok = [[] for _ in range(n)]
        self.i = 0
        self.n = n
        self.held = set()
        self.inuse = set()

    def get(self, P, hold=False):
        for _ in range(self.n):
            i = self.i
            self.i = (self.i + 1) % self.n
            if i in self.held or i in self.inuse:
                continue
            self.last_prior = list(self.tok[i])
            P.wait("tensor", self.tok[i])
            if hold:
                self.held.add(i)
            self.inuse.add(i)
            return self.banks[i], i
        raise RuntimeError("all psum banks busy (held=%s inuse=%s)" % (self.held, self.inuse))

    def release(self, i, toks):
        self.tok[i] = [t for t in toks if t is not None]
        self.held.discard(i)
        self.inuse.discard(i)


class Env:
    def __init__(self, nc, P):
        self.nc = nc
        self.P = P
        self.dram = {}
        self.sb = {}
        self.pool = None
        self.stream_tok = []
        self.const_tok = []
        self.ln_prev = []
        self.wstate = {}
        self.cmap = {}

    def constf(self, v):
        if v not in self.cmap:
            t = self.nc.alloc_sbuf_tensor(f"cst{len(self.cmap)}", [128, 1], F32)
            tok = self.P.emit("vector", lambda e, t=t, v=v: e.memset(t[:, :], v), inc="dve")
            self.cmap[v] = (t[:, :], tok)
        return self.cmap[v]


def _ap(t, off, dims):
    return bass.AP(t, off, dims)



def declare_common(nc, env):
    env.dram["x_in"] = nc.dram_tensor("x_in", [D, T], F32, kind="ExternalInput")
    env.dram["y_out"] = nc.dram_tensor("y_out", [D, T], F32, kind="ExternalOutput")
    env.dram["cid"] = nc.dram_tensor("cid", [128, 128], F32, kind="ExternalInput")
    env.dram["cmask"] = nc.dram_tensor("cmask", [128, 128], F32, kind="ExternalInput")


def emit_prelude(env):
    nc, P = env.nc, env.P
    env.sb["stream"] = nc.alloc_sbuf_tensor("stream", [128, NCH * T], F32)
    env.sb["onesb"] = nc.alloc_sbuf_tensor("onesb", [128, 1], BF16)
    env.sb["one1f"] = nc.alloc_sbuf_tensor("one1f", [1, 128], F32)
    env.sb["ident"] = nc.alloc_sbuf_tensor("ident", [128, 128], BF16)
    env.sb["cmask"] = nc.alloc_sbuf_tensor("cmaskb", [128, 128], F32)
    env.sb["rows"] = nc.alloc_sbuf_tensor("rows", [1, 2048], F32)
    if DBGA:
        env.sb["dbg_pns"] = nc.alloc_sbuf_tensor("dbg_pns", [128, 1024], F32)
    env.sb["dpadv"] = nc.alloc_sbuf_tensor("dpadv", [128, 320], F32)
    env.sb["dpads"] = nc.alloc_sbuf_tensor("dpads", [128, 320], F32)
    P.set_pads(env.sb["dpadv"][:, :], env.sb["dpads"][:, :])
    P.emit("vector", lambda e: e.memset(env.sb["dpadv"][:, :], 0.0))
    P.emit("vector", lambda e: e.memset(env.sb["dpads"][:, :], 0.0))

    stream = env.sb["stream"]
    x = env.dram["x_in"]
    src = _ap(x, 0, [[T, 128], [128 * T, NCH], [1, T]])
    dst = _ap(stream, 0, [[NCH * T, 128], [T, NCH], [1, T]])
    t_x = P.dma("sync", dst, src, "qx")
    env.stream_tok = [t_x]

    t1 = P.emit("vector", lambda e: e.memset(env.sb["onesb"][:, :], 1.0), inc="dve")
    t2 = P.emit("vector", lambda e: e.memset(env.sb["one1f"][:, :], 1.0), inc="dve")
    t3 = P.dma("gpsimd", env.sb["ident"][:, :], env.dram["cid"].ap()[:, :], "qw")
    t4 = P.dma("sync", env.sb["cmask"][:, :], env.dram["cmask"].ap()[:, :], "qx")
    env.const_tok = [t1, t2, t3, t4]
    env.constf(LN_EPS)
    env.constf(1e-4)


def emit_epilogue(env):
    nc, P = env.nc, env.P
    stream = env.sb["stream"]
    y = env.dram["y_out"]
    src = _ap(stream, 0, [[NCH * T, 128], [T, NCH], [1, T]])
    dst = _ap(y, 0, [[T, 128], [128 * T, NCH], [1, T]])
    P.wait("sync", env.stream_tok)
    t_s = P.dma("sync", dst, src, "qx")
    P.wait("sync", [t_s])
    for q in ("qw", "qx"):
        P.wait("sync", P.queue_flush_tokens(q))


def stream_slice(env, fc, col0, ncols):
    return env.sb["stream"][:, fc * T + col0: fc * T + col0 + ncols]


def stream_multi_ap(env, col0, ncols, nch=NCH):
    return _ap(env.sb["stream"], col0, [[NCH * T, 128], [T, nch], [1, ncols]])



def emit_linear(env, *, wname, woff=0, K, O, rhs_sb, rhs_off=0, rhs_pitch, ncols,
                evac, rhs_tok, wbuf, nwin=1, win_stride=0):
    nc, P = env.nc, env.P
    nkc = K // 128
    noc = O // 128
    wd = env.dram[wname]
    blk = nkc * 128
    slot_tok = env.wstate.setdefault(wbuf.name, [None, None])
    wtok = [None, None]

    def issue_w(oc):
        slot = oc % 2
        P.wait("gpsimd", [slot_tok[slot]])
        src = _ap(wd, woff + oc * 128, [[O, 128], [128 * O, nkc], [1, 128]])
        dst = _ap(wbuf, slot * blk, [[2 * blk, 128], [128, nkc], [1, 128]])
        wtok[slot] = P.dma("gpsimd", dst, src, "qw")

    issue_w(0)
    if noc > 1:
        issue_w(1)
    for oc in range(noc):
        slot = oc % 2
        t = None
        for w in range(nwin):
            ps, bi = env.pool.get(P)
            P.wait("tensor", [wtok[slot]] + list(rhs_tok))
            for kc in range(nkc):
                lhs = wbuf[:, slot * blk + kc * 128: slot * blk + (kc + 1) * 128]
                r0 = rhs_off + w * win_stride + kc * rhs_pitch
                rhs = rhs_sb[:, r0: r0 + ncols]
                inc = "pe" if kc == nkc - 1 else None
                t = P.emit("tensor",
                           lambda e, o=ps[:, 0:ncols], l=lhs, r=rhs, s=(kc == 0), p=(kc == nkc - 1):
                           e.matmul(o, l, r, start=s, stop=p), inc=inc)
            if w == nwin - 1:
                slot_tok[slot] = t
                if oc + 2 < noc:
                    issue_w(oc + 2)
            rd = evac(oc, w, ps, t)
            env.pool.release(bi, rd)



def ln_core(env, src_sb, src_off, src_pitch, asq, ncols, waits, eps=LN_EPS):
    nc, P = env.nc, env.P
    rows = env.sb["rows"]
    onesb = env.sb["onesb"]
    one1f = env.sb["one1f"]
    n = ncols
    src_multi = _ap(src_sb, src_off, [[src_sb.shape[1], 128], [src_pitch, NCH], [1, ncols]])
    P.wait("scalar", waits)
    t_asq = P.emit("scalar", lambda e, s=src_multi: e.activation(asq[:, 0:NCH * ncols], s,
                                                                 AF.Square), inc="act")
    psA, biA = env.pool.get(P)
    psB, biB = env.pool.get(P)
    P.wait("tensor", waits + [t_asq] + env.const_tok)
    for kc in range(NCH):
        P.emit("tensor", lambda e, o=psA[0:1, 0:n],
               r=src_sb[:, src_off + kc * src_pitch: src_off + kc * src_pitch + ncols],
               s=(kc == 0), p=(kc == NCH - 1): e.matmul(o, onesb[:, :], r, start=s, stop=p))
    tmm = None
    for kc in range(NCH):
        inc = "pe" if kc == NCH - 1 else None
        tmm = P.emit("tensor", lambda e, o=psB[0:1, 0:n],
                     r=asq[:, kc * ncols:(kc + 1) * ncols],
                     s=(kc == 0), p=(kc == NCH - 1): e.matmul(o, onesb[:, :], r, start=s, stop=p),
                     inc=inc)
    r_nmu = rows[:, 0:512]
    r_var = rows[:, 512:1024]
    r_b = rows[:, 1024:1536]
    r_t = rows[:, 1536:2048]
    P.wait("vector", [tmm] + env.ln_prev)
    P.pause("vector")
    P.emit("vector", lambda e: e.tensor_scalar(r_nmu[:, 0:n], psA[0:1, 0:n], -1.0 / D, None,
                                               ALU.mult))
    P.emit("vector", lambda e: e.tensor_scalar(r_var[:, 0:n], psB[0:1, 0:n], 1.0 / D, None,
                                               ALU.mult))
    P.pause("vector")
    P.emit("vector", lambda e: e.tensor_tensor(r_t[:, 0:n], r_nmu[:, 0:n], r_nmu[:, 0:n],
                                               ALU.mult))
    P.pause("vector")
    t_var = P.emit("vector", lambda e: e.tensor_tensor(r_var[:, 0:n], r_var[:, 0:n],
                                                       r_t[:, 0:n], ALU.subtract), inc="dve")
    env.pool.release(biA, [t_var])
    env.pool.release(biB, [t_var])
    eps_ap, eps_tok = env.constf(eps)
    P.wait("scalar", [t_var, eps_tok])
    P.pause("scalar")
    P.emit("scalar", lambda e: e.activation(r_var[:, 0:n], r_var[:, 0:n], AF.Ln,
                                            bias=eps_ap[0:1, 0:1]))
    t_rstd = P.emit("scalar", lambda e: e.activation(r_var[:, 0:n], r_var[:, 0:n], AF.Exp,
                                                     scale=-0.5), inc="act")
    P.wait("vector", [t_rstd])
    P.pause("vector")
    t_b = P.emit("vector", lambda e: e.tensor_tensor(r_b[:, 0:n], r_nmu[:, 0:n],
                                                     r_var[:, 0:n], ALU.mult), inc="dve")
    P.pause("vector")

    psBA, biBA = env.pool.get(P)
    psBB, biBB = env.pool.get(P)
    P.wait("tensor", [t_rstd, t_b] + env.const_tok)
    P.pause("tensor")
    P.emit("tensor", lambda e: e.matmul(psBA[:, 0:n], one1f[:, :], r_var[:, 0:n],
                                        start=True, stop=True))
    t_bc = P.emit("tensor", lambda e: e.matmul(psBB[:, 0:n], one1f[:, :], r_b[:, 0:n],
                                               start=True, stop=True), inc="pe")
    env.ln_prev = [t_bc, t_rstd]
    return psBA, psBB, biBA, biBB, t_bc


def emit_postln(env, col0, ncols, abf, asq):
    nc, P = env.nc, env.P
    srcap = stream_multi_ap(env, col0, ncols)
    P.wait("vector", env.stream_tok + env.const_tok)
    t_abf = P.emit("vector", lambda e, s=srcap: e.tensor_copy(abf[:, 0:NCH * ncols], s),
                   inc="dve")
    psBA, psBB, biBA, biBB, t_bc = ln_core(env, abf, 0, ncols, asq, ncols,
                                           env.stream_tok + [t_abf])
    n = ncols
    P.wait("vector", [t_bc])
    P.pause("vector")
    toks = []
    for fc in range(NCH):
        sl = stream_slice(env, fc, col0, ncols)
        P.emit("vector", lambda e, sl=sl: e.tensor_tensor(sl, sl, psBA[:, 0:n], ALU.mult))
        tk = P.emit("vector", lambda e, sl=sl: e.tensor_tensor(sl, sl, psBB[:, 0:n], ALU.add),
                    inc="dve")
        toks.append(tk)
    env.pool.release(biBA, toks)
    env.pool.release(biBB, toks)
    return toks



def declare_l0ffn(nc, env):
    env.dram["f_w1T"] = nc.dram_tensor("f_w1T", [D, 4096], F32, kind="ExternalInput")
    env.dram["f_w2T"] = nc.dram_tensor("f_w2T", [D, 4096], F32, kind="ExternalInput")
    env.dram["f_h1T"] = nc.dram_tensor("f_h1T", [4096, 4096], F32, kind="ExternalInput")
    env.dram["f_h2T"] = nc.dram_tensor("f_h2T", [4096, 4096], F32, kind="ExternalInput")
    env.dram["f_w3T"] = nc.dram_tensor("f_w3T", [4096, D], F32, kind="ExternalInput")
    env.dram["f_s1"] = nc.dram_tensor("f_s1", [128, 32], F32, kind="ExternalInput")
    env.dram["f_s2"] = nc.dram_tensor("f_s2", [128, 32], F32, kind="ExternalInput")
    env.dram["f_sh1"] = nc.dram_tensor("f_sh1", [128, 32], F32, kind="ExternalInput")
    env.dram["f_sh2"] = nc.dram_tensor("f_sh2", [128, 32], F32, kind="ExternalInput")
    env.dram["f_s3"] = nc.dram_tensor("f_s3", [128, 8], F32, kind="ExternalInput")


def emit_l0ffn(env):
    nc, P = env.nc, env.P
    with ExitStack() as es:
        XQ = es.enter_context(nc.sbuf_tensor("ff_xq", [128, NCH * TT], BF16))
        HB = es.enter_context(nc.sbuf_tensor("ff_hb", [128, 32 * TT], BF16))
        GB = es.enter_context(nc.sbuf_tensor("ff_gb", [128, 32 * TT], BF16))
        SIL = es.enter_context(nc.sbuf_tensor("ff_sil", [128, 2 * TT], F32))
        W8 = es.enter_context(nc.sbuf_tensor("ff_w8", [128, 2 * 8 * 128], BF16))
        W8b = es.enter_context(nc.sbuf_tensor("ff_w8b", [128, 2 * 8 * 128], BF16))
        W32 = es.enter_context(nc.sbuf_tensor("ff_w32", [128, 2 * 32 * 128], BF16))
        ABF = es.enter_context(nc.sbuf_tensor("ff_abf", [128, NCH * TT], BF16))
        ASQ = es.enter_context(nc.sbuf_tensor("ff_asq", [128, NCH * TT], BF16))
        SC = {}
        sc_tok = []
        for nm, w in (("s1", 32), ("s2", 32), ("sh1", 32), ("sh2", 32), ("s3", 8)):
            SC[nm] = es.enter_context(nc.sbuf_tensor("ff_" + nm, [128, w], F32))
            sc_tok.append(P.dma("sync", SC[nm][:, :], env.dram["f_" + nm].ap()[:, :], "qx"))

        wtok1 = [None, None]
        wtok2 = [None, None]
        slot_busy = [None, None]
        sil_tok = [None, None]
        for q in range(NTT):
            col0 = q * TT
            P.wait("vector", env.stream_tok + [slot_busy[0], slot_busy[1]])
            t_xq = P.emit("vector", lambda e, s=stream_multi_ap(env, col0, TT):
                          e.tensor_copy(XQ[:, :], s), inc="dve")

            nkc, noc = NCH, 32

            def issue_w(oc):
                slot = oc % 2
                P.wait("gpsimd", [slot_busy[slot]])
                for wn, buf, arr in (("f_w1T", W8, wtok1), ("f_w2T", W8b, wtok2)):
                    wd = env.dram[wn]
                    src = _ap(wd, oc * 128, [[4096, 128], [128 * 4096, nkc], [1, 128]])
                    dst = _ap(buf, slot * nkc * 128, [[2 * nkc * 128, 128], [128, nkc], [1, 128]])
                    arr[slot] = P.dma("gpsimd", dst, src, "qw")

            issue_w(0)
            issue_w(1)
            for oc in range(noc):
                slot = oc % 2
                ps1, b1 = env.pool.get(P)
                ps2, b2 = env.pool.get(P)
                P.wait("tensor", [wtok1[slot], wtok2[slot], t_xq] + sc_tok)
                tm1 = tm2 = None
                for kc in range(nkc):
                    rhs = XQ[:, kc * TT:(kc + 1) * TT]
                    lhs1 = W8[:, slot * nkc * 128 + kc * 128: slot * nkc * 128 + (kc + 1) * 128]
                    tm1 = P.emit("tensor", lambda e, o=ps1[:, :], l=lhs1, r=rhs, s=(kc == 0),
                                 p=(kc == nkc - 1): e.matmul(o, l, r, start=s, stop=p),
                                 inc="pe" if kc == nkc - 1 else None)
                for kc in range(nkc):
                    lhs2 = W8b[:, slot * nkc * 128 + kc * 128: slot * nkc * 128 + (kc + 1) * 128]
                    rhs = XQ[:, kc * TT:(kc + 1) * TT]
                    tm2 = P.emit("tensor", lambda e, o=ps2[:, :], l=lhs2, r=rhs, s=(kc == 0),
                                 p=(kc == nkc - 1): e.matmul(o, l, r, start=s, stop=p),
                                 inc="pe" if kc == nkc - 1 else None)
                slot_busy[slot] = tm2
                if oc + 2 < noc:
                    issue_w(oc + 2)
                ss = oc % 2
                P.wait("scalar", [tm1, sil_tok[ss]])
                t_sil = P.emit("scalar", lambda e, o=SIL[:, ss * TT:(ss + 1) * TT], i=ps1[:, :],
                               sc=SC["s1"][:, oc:oc + 1]: e.activation(o, i, AF.Silu, scale=sc),
                               inc="act")
                P.wait("vector", [tm2, t_sil])
                t_hb = P.emit("vector", lambda e, o=HB[:, oc * TT:(oc + 1) * TT], i=ps2[:, :],
                              sc=SC["s2"][:, oc:oc + 1], s=SIL[:, ss * TT:(ss + 1) * TT]:
                              e.scalar_tensor_tensor(o, i, sc, s, ALU.mult, ALU.mult), inc="dve")
                sil_tok[ss] = t_hb
                env.pool.release(b1, [t_sil])
                env.pool.release(b2, [t_hb])
            t_h = [sil_tok[0], sil_tok[1]]

            s2_toks = []

            def evac_s2(oc, w, ps, tmm, dst=GB, sc="sh1"):
                P.wait("scalar", [tmm])
                tk = P.emit("scalar", lambda e, o=dst[:, oc * TT:(oc + 1) * TT], i=ps[:, :],
                            s=SC[sc][:, oc:oc + 1]: e.activation(o, i, AF.Gelu, scale=s),
                            inc="act")
                s2_toks.append(tk)
                return [tk]

            emit_linear(env, wname="f_h1T", K=4096, O=4096, rhs_sb=HB,
                        rhs_pitch=TT, ncols=TT, evac=evac_s2, rhs_tok=t_h, wbuf=W32)

            s3_toks = []

            def evac_s3(oc, w, ps, tmm):
                P.wait("scalar", [tmm])
                tk = P.emit("scalar", lambda e, o=HB[:, oc * TT:(oc + 1) * TT], i=ps[:, :],
                            s=SC["sh2"][:, oc:oc + 1]: e.activation(o, i, AF.Gelu, scale=s),
                            inc="act")
                s3_toks.append(tk)
                return [tk]

            emit_linear(env, wname="f_h2T", K=4096, O=4096, rhs_sb=GB,
                        rhs_pitch=TT, ncols=TT, evac=evac_s3, rhs_tok=s2_toks[-2:], wbuf=W32)

            s4_toks = []

            def evac_s4(oc, w, ps, tmm):
                sl = stream_slice(env, oc, col0, TT)
                P.wait("vector", [tmm] + env.stream_tok)
                tk = P.emit("vector", lambda e, o=sl, i=ps[:, :], s=SC["s3"][:, oc:oc + 1]:
                            e.scalar_tensor_tensor(o, i, s, o, ALU.mult, ALU.add), inc="dve")
                s4_toks.append(tk)
                return [tk]

            emit_linear(env, wname="f_w3T", K=4096, O=D, rhs_sb=HB,
                        rhs_pitch=TT, ncols=TT, evac=evac_s4, rhs_tok=s3_toks[-2:], wbuf=W32)

            env.stream_tok = env.stream_tok + s4_toks
            ln_toks = emit_postln(env, col0, TT, ABF, ASQ)
            env.stream_tok = ln_toks



DBGA = False


def declare_l0attn(nc, env):
    if DBGA:
        env.dram["dbg_pn"] = nc.dram_tensor("dbg_pn", [128, 1024], F32, kind="ExternalOutput")
        env.dram["dbg_ssb"] = nc.dram_tensor("dbg_ssb", [128, 16 * 65], mybir.dt.bfloat16,
                                             kind="ExternalOutput")
    env.dram["a_wqkvT"] = nc.dram_tensor("a_wqkvT", [D, 3 * D], F32, kind="ExternalInput")
    env.dram["a_woutT"] = nc.dram_tensor("a_woutT", [D, D], F32, kind="ExternalInput")
    env.dram["a_bqkv"] = nc.dram_tensor("a_bqkv", [128, 24], F32, kind="ExternalInput")
    env.dram["a_bout"] = nc.dram_tensor("a_bout", [128, 8], F32, kind="ExternalInput")
    env.dram["a_gamma"] = nc.dram_tensor("a_gamma", [128, 8], F32, kind="ExternalInput")
    env.dram["a_beta"] = nc.dram_tensor("a_beta", [128, 8], F32, kind="ExternalInput")


def emit_l0attn(env):
    nc, P = env.nc, env.P
    FE = 1e-4
    with ExitStack() as es:
        XT = es.enter_context(nc.sbuf_tensor("la_xt", [128, NCH * TT], BF16))
        QF = es.enter_context(nc.sbuf_tensor("la_qf", [128, NCH * TT], BF16))
        KF = es.enter_context(nc.sbuf_tensor("la_kf", [128, NCH * TT], BF16))
        KT = es.enter_context(nc.sbuf_tensor("la_kt", [128, 4 * 1024], BF16))
        VT = es.enter_context(nc.sbuf_tensor("la_vt", [128, 4 * 1024], BF16))
        VS = es.enter_context(nc.sbuf_tensor("la_vs", [128, 2 * TT], BF16))
        OF = es.enter_context(nc.sbuf_tensor("la_of", [128, NCH * TT], BF16))
        A0 = es.enter_context(nc.sbuf_tensor("la_a0", [128, NCH * TT], BF16))
        MB = es.enter_context(nc.sbuf_tensor("la_mb", [128, 16 * 128], BF16))
        SSB = es.enter_context(nc.sbuf_tensor("la_ssb", [128, 16 * 65], BF16))
        QSC = es.enter_context(nc.sbuf_tensor("la_qsc", [128, 2 * TT], F32))
        BCS = es.enter_context(nc.sbuf_tensor("la_bcs", [128, 2 * TT], F32))
        TMPR = es.enter_context(nc.sbuf_tensor("la_tmpr", [128, TT], F32))
        ABF = es.enter_context(nc.sbuf_tensor("la_abf", [128, NCH * TT], BF16))
        ASQ = es.enter_context(nc.sbuf_tensor("la_asq", [128, NCH * TT], BF16))
        W8A = es.enter_context(nc.sbuf_tensor("la_w8", [128, 2 * NCH * 128], BF16))
        BQ = es.enter_context(nc.sbuf_tensor("la_bq", [128, 24], F32))
        BO = es.enter_context(nc.sbuf_tensor("la_bo", [128, 8], F32))
        GAM = es.enter_context(nc.sbuf_tensor("la_gam", [128, 8], F32))
        BET = es.enter_context(nc.sbuf_tensor("la_bet", [128, 8], F32))
        PTA = es.enter_context(nc.psum_tensor("la_pta", [128, 256], BF16))
        PTB = es.enter_context(nc.psum_tensor("la_ptb", [128, 256], BF16))
        pt_rd = [None]

        def pt_of(tc):
            return (PTA if tc < 2 else PTB), (tc % 2) * 128

        t_bq = P.dma("sync", BQ[:, :], env.dram["a_bqkv"].ap()[:, :], "qx")
        t_bo = P.dma("sync", BO[:, :], env.dram["a_bout"].ap()[:, :], "qx")
        t_gm = P.dma("sync", GAM[:, :], env.dram["a_gamma"].ap()[:, :], "qx")
        t_bt = P.dma("sync", BET[:, :], env.dram["a_beta"].ap()[:, :], "qx")
        t_ssz = P.emit("vector", lambda e: e.memset(SSB[:, :], 0.0), inc="dve")
        rows = env.sb["rows"]

        PSA, psa_i = env.pool.get(P, hold=True)
        prA = env.pool.last_prior
        PSB, psb_i = env.pool.get(P, hold=True)
        prB = env.pool.last_prior
        P.wait("vector", prA + prB)
        P.emit("vector", lambda e: e.memset(PSA[:, :], 0.0))
        t_psz = P.emit("vector", lambda e: e.memset(PSB[:, :], 0.0), inc="dve")
        s_evac_tok = [t_ssz]
        s_last_pe = [None]

        xt_reader = [None]
        rows_reader_pe = [None]

        def s_bank(h):
            bank = PSA if h < 8 else PSB
            hh = h % 8
            return bank, (h % 2) * 64, ((hh // 2) % 4) * 80

        for ti in range(NTT):
            col0 = ti * TT
            P.wait("vector", env.stream_tok + [xt_reader[0]])
            t_xt = P.emit("vector", lambda e, s=stream_multi_ap(env, col0, TT):
                          e.tensor_copy(XT[:, :], s), inc="dve")
            P.pause("vector")

            q_tok = [None]
            k_tok = [None]
            vt_tok = [None]
            qsc_rd = [None]
            vs_rd = [None, None]

            def evac_qkv(oc, w, ps, tmm):
                if oc < 16:
                    P.wait("scalar", [tmm, t_bq, qsc_rd[0]])
                    t1 = P.emit("scalar", lambda e, o=QSC[:, 0:TT], i=ps[:, 0:TT],
                                b=BQ[:, oc:oc + 1]: e.activation(o, i, AF.Exp, bias=b),
                                inc="act")
                    P.wait("vector", [tmm, t1, t_bq])
                    P.emit("vector", lambda e, o=QSC[:, TT:2 * TT], i=ps[:, 0:TT],
                           b=BQ[:, oc:oc + 1]: e.tensor_scalar(o, i, b, 0.0, ALU.add, ALU.max))
                    P.emit("vector", lambda e: e.tensor_scalar(QSC[:, 0:TT], QSC[:, 0:TT],
                                                               1.0, None, ALU.min))
                    if oc < 8:
                        P.emit("vector", lambda e: e.scalar_tensor_tensor(
                            QSC[:, 0:TT], QSC[:, 0:TT], FE, QSC[:, TT:2 * TT], ALU.add, ALU.add))
                        tk = P.emit("vector", lambda e, o=QF[:, oc * TT:(oc + 1) * TT]:
                                    e.tensor_scalar(o, QSC[:, 0:TT], 0.125, None, ALU.mult),
                                    inc="dve")
                        q_tok[0] = tk
                    else:
                        tk = P.emit("vector", lambda e, o=KF[:, (oc - 8) * TT:(oc - 7) * TT]:
                                    e.scalar_tensor_tensor(o, QSC[:, 0:TT], FE, QSC[:, TT:2 * TT],
                                                           ALU.add, ALU.add), inc="dve")
                        k_tok[0] = tk
                    qsc_rd[0] = tk
                    return [t1, tk]
                vc = oc - 16
                slot = vc % 2
                P.wait("vector", [tmm, t_bq, vs_rd[slot]])
                P.emit("vector", lambda e, o=VS[:, slot * TT:(slot + 1) * TT], i=ps[:, 0:TT],
                       b=BQ[:, oc:oc + 1]: e.tensor_scalar(o, i, b, None, ALU.add))
                t_vs = P.settle("vector", "dve")
                P.wait("tensor", [t_vs, pt_rd[0]] + env.const_tok)
                t_tr = None
                for tc in range(4):
                    PTt, o0 = pt_of(tc)
                    t_tr = P.emit("tensor", lambda e, o=PTt[:, o0:o0 + 128],
                                  i=VS[:, slot * TT + tc * 128: slot * TT + (tc + 1) * 128]:
                                  e.transpose(o, i, env.sb["ident"][:, :]),
                                  inc="pe" if tc == 3 else None)
                P.wait("scalar", [t_tr])
                tv = None
                for tc in range(4):
                    PTt, o0 = pt_of(tc)
                    tv = P.emit("scalar", lambda e,
                                d=VT[:, tc * 1024 + vc * 128: tc * 1024 + (vc + 1) * 128],
                                s=PTt[:, o0:o0 + 128]:
                                e.copy(d, s), inc="act" if tc == 3 else None)
                vt_tok[0] = tv
                vs_rd[slot] = t_tr
                pt_rd[0] = tv
                return [tv]

            emit_linear(env, wname="a_wqkvT", K=D, O=3 * D, rhs_sb=XT, rhs_pitch=TT,
                        ncols=TT, evac=evac_qkv, rhs_tok=[t_xt], wbuf=W8A)
            xt_reader[0] = ("pe", P.cnt["pe"])

            t_kf = P.settle("vector", "dve")
            t_kt = [None]
            for fc in range(NCH):
                P.wait("tensor", [t_kf, k_tok[0], pt_rd[0]] + env.const_tok)
                t_tr = None
                for tc in range(4):
                    PTt, o0 = pt_of(tc)
                    t_tr = P.emit("tensor", lambda e, o=PTt[:, o0:o0 + 128],
                                  i=KF[:, fc * TT + tc * 128: fc * TT + (tc + 1) * 128]:
                                  e.transpose(o, i, env.sb["ident"][:, :]),
                                  inc="pe" if tc == 3 else None)
                P.wait("scalar", [t_tr])
                for tc in range(4):
                    PTt, o0 = pt_of(tc)
                    t_kt[0] = P.emit("scalar", lambda e,
                                     d=KT[:, tc * 1024 + fc * 128: tc * 1024 + (fc + 1) * 128],
                                     s=PTt[:, o0:o0 + 128]:
                                     e.copy(d, s), inc="act" if tc == 3 else None)
                pt_rd[0] = t_kt[0]
            t_ktvt = P.settle("scalar", "act")

            for tc in range(4):
                pms = [env.pool.get(P) for _ in range(4)]
                P.wait("tensor", [q_tok[0], k_tok[0], t_kf])
                t_m = None
                for h in range(16):
                    pmb = pms[(h % 2) * 2 + (h // 8)][0]
                    mcol = ((h // 2) % 4) * 128
                    r0 = (h % 2) * 64
                    fcc = h // 2
                    t_m = P.emit("tensor", lambda e,
                                 o=pmb[:, mcol:mcol + 128],
                                 l=KF[r0:r0 + 64, fcc * TT + tc * 128: fcc * TT + (tc + 1) * 128],
                                 r=QF[r0:r0 + 64, fcc * TT + tc * 128: fcc * TT + (tc + 1) * 128],
                                 :
                                 e.matmul(o, l, r, start=True, stop=True,
                                          skip_group_check=True),
                                 inc="pe" if h == 15 else None)
                P.wait("vector", [t_m] + env.const_tok)
                t_mb = None
                for h in range(16):
                    pmb = pms[(h % 2) * 2 + (h // 8)][0]
                    mcol = ((h // 2) % 4) * 128
                    t_mb = P.emit("vector", lambda e, o=MB[:, h * 128:(h + 1) * 128],
                                  i=pmb[:, mcol:mcol + 128]:
                                  e.scalar_tensor_tensor(o, i, 1.0, env.sb["cmask"][:, :],
                                                         ALU.mult, ALU.mult),
                                  inc="dve" if h == 15 else None)
                t_mbs = P.settle("vector", "dve")
                for _, pi in pms:
                    env.pool.release(pi, [t_mbs])

                pn = [env.pool.get(P) for _ in range(2)]
                pd = [env.pool.get(P) for _ in range(2)]
                P.wait("tensor", [t_mbs, t_ktvt] + s_evac_tok)
                t_num = None
                for h in range(16):
                    pnb = pn[h // 8][0]
                    r0 = (h % 2) * 64
                    c0 = ((h // 2) % 4) * 128
                    fcc = h // 2
                    vcol = tc * 1024 + fcc * 128 + (h % 2) * 64
                    first = (h % 8 == 0)
                    P.emit("tensor", lambda e, o=pnb[r0:r0 + 64, c0:c0 + 128],
                           l=VT[:, vcol:vcol + 64], r=MB[:, h * 128:(h + 1) * 128],
                           tp=(0, r0):
                           e.matmul(o, l, r, start=True, stop=False, skip_group_check=True,
                                    tile_position=tp))
                    t_num = P.emit("tensor", lambda e, o=pnb[r0:r0 + 64, c0:c0 + 128],
                                   l=SSB[:, h * 65: h * 65 + 64],
                                   r=QF[:, fcc * TT + tc * 128: fcc * TT + (tc + 1) * 128],
                                   tp=(0, r0):
                                   e.matmul(o, l, r, start=False, stop=True,
                                            skip_group_check=True, tile_position=tp),
                                   inc="pe" if h == 15 else None)
                t_den = None
                for h in range(16):
                    pdb = pd[h // 8][0]
                    r0 = 32 * (h % 2)
                    c0 = ((h // 2) % 4) * 128
                    fcc = h // 2
                    first = (h % 8 == 0)
                    P.emit("tensor", lambda e, o=pdb[r0:r0 + 1, c0:c0 + 128],
                           r=MB[:, h * 128:(h + 1) * 128], tp=(0, r0):
                           e.matmul(o, env.sb["onesb"][:, :], r, start=True, stop=False,
                                    skip_group_check=True, tile_position=tp))
                    t_den = P.emit("tensor", lambda e, o=pdb[r0:r0 + 1, c0:c0 + 128],
                                   l=SSB[:, h * 65 + 64: h * 65 + 65],
                                   r=QF[:, fcc * TT + tc * 128: fcc * TT + (tc + 1) * 128],
                                   tp=(0, r0):
                                   e.matmul(o, l, r, start=False, stop=True,
                                            skip_group_check=True, tile_position=tp),
                                   inc="pe" if h == 15 else None)

                t_su = None
                if ti == 0 and tc == 0:
                    P.wait("tensor", [t_psz])
                for h in range(16):
                    sb_, sr, sc = s_bank(h)
                    kcol = tc * 1024 + (h // 2) * 128 + (h % 2) * 64
                    P.emit("tensor", lambda e, o=sb_[sr:sr + 64, sc:sc + 64],
                           l=KT[:, kcol:kcol + 64], r=VT[:, kcol:kcol + 64],
                           tp=(0, sr):
                           e.matmul(o, l, r, start=False, stop=False, skip_group_check=True,
                                    tile_position=tp))
                    t_su = P.emit("tensor", lambda e, o=sb_[sr:sr + 64, sc + 64:sc + 65],
                                  l=KT[:, kcol:kcol + 64], r=env.sb["onesb"][:, :], tp=(0, sr):
                                  e.matmul(o, l, r, start=False, stop=True,
                                           skip_group_check=True, tile_position=tp),
                                  inc="pe" if h == 15 else None)
                s_last_pe[0] = t_su

                P.wait("scalar", [t_den] + env.ln_prev)
                t_de = None
                for b in range(2):
                    for par in range(2):
                        t_de = P.emit("scalar", lambda e,
                                      d=_ap(rows, (b * 8 + par) * 128, [[2048, 1], [256, 4], [1, 128]]),
                                      s=_ap(pd[b][0], 32 * par * 512, [[512, 1], [128, 4], [1, 128]]):
                                      e.copy(d, s), inc="act" if (b, par) == (1, 1) else None)
                t_des = P.settle("scalar", "act")
                for _, pi in pd:
                    env.pool.release(pi, [t_des])
                P.wait("vector", [t_des])
                P.pause("vector")
                P.emit("vector", lambda e: e.tensor_scalar(rows[:, 0:2048], rows[:, 0:2048],
                                                           FE, None, ALU.add))
                P.pause("vector")
                t_rd = P.emit("vector", lambda e: e.reciprocal(rows[:, 0:2048], rows[:, 0:2048]),
                              inc="dve")
                t_rds = P.settle("vector", "dve")
                pbc = [env.pool.get(P) for _ in range(2)]
                P.wait("tensor", [t_rds] + env.const_tok)
                t_pb = None
                for h in range(16):
                    pbb = pbc[h // 8][0]
                    r0 = (h % 2) * 64
                    c0 = ((h // 2) % 4) * 128
                    first = (h % 8 == 0)
                    t_pb = P.emit("tensor", lambda e, o=pbb[r0:r0 + 64, c0:c0 + 128],
                                  l=env.sb["one1f"][0:1, 0:64],
                                  r=rows[0:1, h * 128:(h + 1) * 128],
                                  tp=(0, r0):
                                  e.matmul(o, l, r, start=True, stop=True,
                                           skip_group_check=True, tile_position=tp),
                                  inc="pe" if h == 15 else None)
                rows_reader_pe[0] = t_pb
                env.ln_prev = env.ln_prev + [t_pb]
                P.wait("scalar", [t_pb])
                t_bcs = None
                for b in range(2):
                    t_bcs = P.emit("scalar", lambda e, d=BCS[:, b * TT:(b + 1) * TT],
                                   s=pbc[b][0][:, 0:TT]: e.copy(d, s),
                                   inc="act" if b == 1 else None)
                t_bcss = P.settle("scalar", "act")
                for _, pi in pbc:
                    env.pool.release(pi, [t_bcss])

                if DBGA and ti == 3 and tc == 3:
                    db = env.sb["dbg_pns"]
                    P.wait("scalar", [t_num])
                    P.emit("scalar", lambda e, d=db[:, 0:512]: e.copy(d, pn[0][0][:, :]))
                    tdb = P.emit("scalar", lambda e, d=db[:, 512:1024]: e.copy(d, pn[1][0][:, :]),
                                 inc="act")
                    P.wait("sync", [tdb] + s_evac_tok)
                    P.dma("sync", env.dram["dbg_pn"].ap()[:, :], db[:, :], "qx")
                    P.dma("sync", env.dram["dbg_ssb"].ap()[:, :], SSB[:, :], "qx")
                P.wait("vector", [t_num, t_bcss])
                t_of = None
                for b in range(2):
                    t_of = P.emit("vector", lambda e,
                                  o=_ap(OF, tc * 128 + b * 4 * TT, [[NCH * TT, 128], [TT, 4], [1, 128]]),
                                  i=_ap(pn[b][0], 0, [[512, 128], [128, 4], [1, 128]]),
                                  c=_ap(BCS, b * TT, [[2 * TT, 128], [128, 4], [1, 128]]):
                                  e.tensor_tensor(o, i, c, ALU.mult),
                                  inc="dve" if b == 1 else None)
                for _, pi in pn:
                    env.pool.release(pi, [t_of])

                P.wait("scalar", [t_su, t_of])
                t_se = None
                for bi_, bank in ((0, PSA), (1, PSB)):
                    for par in range(2):
                        t_se = P.emit("scalar", lambda e,
                                      d=_ap(SSB, par * 64 * (16 * 65) + (bi_ * 8 + par) * 65,
                                            [[16 * 65, 64], [130, 4], [1, 65]]),
                                      s=_ap(bank, par * 64 * 512, [[512, 64], [80, 4], [1, 65]]):
                                      e.copy(d, s), inc="act" if (bi_, par) == (1, 1) else None)
                s_evac_tok = [P.settle("scalar", "act")]

            t_ofs = P.settle("vector", "dve")
            a0_tok = [None]

            def evac_out(oc, w, ps, tmm):
                P.wait("vector", [tmm, t_bo])
                tk = P.emit("vector", lambda e, o=A0[:, oc * TT:(oc + 1) * TT], i=ps[:, 0:TT],
                            b=BO[:, oc:oc + 1]: e.tensor_scalar(o, i, b, None, ALU.add),
                            inc="dve")
                a0_tok[0] = tk
                return [tk]

            emit_linear(env, wname="a_woutT", K=D, O=D, rhs_sb=OF, rhs_pitch=TT,
                        ncols=TT, evac=evac_out, rhs_tok=[t_ofs], wbuf=W8A)

            psBA, psBB, biBA, biBB, t_bc = ln_core(env, A0, 0, TT, ASQ, TT, [a0_tok[0]])
            P.wait("vector", [t_bc, t_gm, t_bt] + env.stream_tok)
            P.pause("vector")
            toks = []
            for fc in range(NCH):
                sl = stream_slice(env, fc, col0, TT)
                a0s = A0[:, fc * TT:(fc + 1) * TT]
                P.emit("vector", lambda e, a=a0s: e.tensor_tensor(TMPR[:, 0:TT], a,
                                                                  psBA[:, 0:TT], ALU.mult))
                P.emit("vector", lambda e: e.tensor_tensor(TMPR[:, 0:TT], TMPR[:, 0:TT],
                                                           psBB[:, 0:TT], ALU.add))
                P.emit("vector", lambda e, g=GAM[:, fc:fc + 1], b=BET[:, fc:fc + 1]:
                       e.tensor_scalar(TMPR[:, 0:TT], TMPR[:, 0:TT], g, b, ALU.mult, ALU.add))
                tk = P.emit("vector", lambda e, sl=sl: e.tensor_tensor(sl, sl, TMPR[:, 0:TT],
                                                                       ALU.add), inc="dve")
                toks.append(tk)
            env.pool.release(biBA, toks)
            env.pool.release(biBB, toks)
            env.stream_tok = toks
            ln_toks = emit_postln(env, col0, TT, ABF, ASQ)
            env.stream_tok = ln_toks

        env.pool.release(psa_i, s_evac_tok)
        env.pool.release(psb_i, s_evac_tok)


def declare_l1attn(nc, env):
    env.dram["d_wqkv"] = nc.dram_tensor("d_wqkv", [D, 3 * D], F32, kind="ExternalInput")
    env.dram["d_wproj"] = nc.dram_tensor("d_wproj", [D, D], F32, kind="ExternalInput")
    env.dram["d_lam"] = nc.dram_tensor("d_lam", [4, 512], F32, kind="ExternalInput")
    env.dram["d_dmask"] = nc.dram_tensor("d_dmask", [128, 4 * 512], F32, kind="ExternalInput")


def emit_l1attn(env):
    nc, P = env.nc, env.P
    LI = 0.8 - math.exp(-0.3) * 0.6
    with ExitStack() as es:
        ATTN = es.enter_context(nc.sbuf_tensor("da_attn", [128, NCH * T], BF16))
        DMK = es.enter_context(nc.sbuf_tensor("da_dmk", [128, 4 * TT], F32))
        BCS2 = es.enter_context(nc.sbuf_tensor("da_bcs", [128, 2 * TT], F32))
        TMPA = es.enter_context(nc.sbuf_tensor("da_tmpa", [128, 2 * TT], F32))
        W8B = es.enter_context(nc.sbuf_tensor("da_w8", [128, 2 * NCH * 128], BF16))
        LAMT = es.enter_context(nc.sbuf_tensor("da_lamt", [1, 8], F32))
        LROW = es.enter_context(nc.sbuf_tensor("da_lrow", [1, 512], F32))
        DROW = es.enter_context(nc.sbuf_tensor("da_drow", [1, 1024], F32))
        LAMV = es.enter_context(nc.sbuf_tensor("da_lamv", [1, 2048], F32))
        ABF = es.enter_context(nc.sbuf_tensor("da_abf", [128, NCH * TT], BF16))
        ASQ = es.enter_context(nc.sbuf_tensor("da_asq", [128, NCH * TT], BF16))
        PTA = es.enter_context(nc.psum_tensor("da_pta", [128, 128], BF16))
        PTB = es.enter_context(nc.psum_tensor("da_ptb", [128, 128], BF16))

        ident = env.sb["ident"]

        t_mk = P.dma("sync", DMK[:, :], env.dram["d_dmask"].ap()[:, :], "qx")
        t_lam = P.dma("sync", LAMV[0:1, :],
                      _ap(env.dram["d_lam"], 0, [[512, 1], [512, 4], [1, 512]]), "qx")

        P.wait("vector", [t_lam])
        P.pause("vector")
        P.emit("vector", lambda e: e.tensor_tensor(LROW[0:1, :], LAMV[0:1, 0:512],
                                                   LAMV[0:1, 512:1024], ALU.mult))
        P.pause("vector")
        P.emit("vector", lambda e: e.reduce_sum(LAMT[0:1, 0:1], LROW[0:1, :], axis=AX.X))
        P.pause("vector")
        P.emit("vector", lambda e: e.tensor_tensor(LROW[0:1, :], LAMV[0:1, 1024:1536],
                                                   LAMV[0:1, 1536:2048], ALU.mult))
        P.pause("vector")
        t_l1 = P.emit("vector", lambda e: e.reduce_sum(LAMT[0:1, 1:2], LROW[0:1, :],
                                                       axis=AX.X), inc="dve")
        P.pause("vector")
        P.wait("scalar", [t_l1])
        P.pause("scalar")
        t_l2 = P.emit("scalar", lambda e: e.activation(LAMT[0:1, 0:2], LAMT[0:1, 0:2],
                                                       AF.Exp), inc="act")
        P.wait("vector", [t_l2])
        P.pause("vector")
        P.emit("vector", lambda e: e.tensor_tensor(LAMT[0:1, 2:3], LAMT[0:1, 0:1],
                                                   LAMT[0:1, 1:2], ALU.subtract))
        P.pause("vector")
        P.emit("vector", lambda e: e.tensor_scalar(LAMT[0:1, 2:3], LAMT[0:1, 2:3],
                                                   LI, None, ALU.add))
        P.pause("vector")
        t_l3 = P.emit("vector", lambda e: e.tensor_scalar(LAMT[0:1, 3:4], LAMT[0:1, 2:3],
                                                          -(1.0 - LI), None, ALU.mult),
                      inc="dve")
        P.pause("vector")

        attn_toks = [None]
        with ExitStack() as es2:
            XBT = es2.enter_context(nc.sbuf_tensor("da_xbt", [128, NCH * TT], BF16))
            QH = es2.enter_context(nc.sbuf_tensor("da_qh", [128, 4 * T], BF16))
            KH = es2.enter_context(nc.sbuf_tensor("da_kh", [128, 2 * T], BF16))
            VF2 = es2.enter_context(nc.sbuf_tensor("da_vf2", [128, T], BF16))
            VH = es2.enter_context(nc.sbuf_tensor("da_vh", [128, 2 * 1040], BF16))
            PAB = es2.enter_context(nc.sbuf_tensor("da_pab", [128, 2 * TT], BF16))
            SMF = es2.enter_context(nc.sbuf_tensor("da_smf", [128, 2 * TT], F32))

            xbt_reader = [None]
            pt_rd = [None]

            for hg in range(4):
                col0 = hg * TT
                P.wait("vector", env.stream_tok + [xbt_reader[0]])
                t_xb = P.emit("vector", lambda e, s=stream_multi_ap(env, col0, TT):
                              e.tensor_copy(XBT[:, :], s), inc="dve")
                P.pause("vector")
                t_vh0 = P.emit("vector", lambda e: e.memset(VH[:, :], 1.0), inc="dve")

                q_tok = [None]
                k_tok = [None]

                def evac_qkv(oc, w, ps, tmm):
                    P.wait("vector", [tmm])
                    tk = None
                    if oc < 16:
                        for rp in range(2):
                            for nlp in range(2):
                                r = 2 * oc + rp
                                tk = P.emit("vector", lambda e,
                                            d=_ap(QH, nlp * 64 * (4 * T) + r,
                                                  [[4 * T, 64], [T, 4], [32, 64]]),
                                            s=_ap(ps, rp * 64 * 512 + nlp * 64,
                                                  [[512, 64], [128, 4], [1, 64]]):
                                            e.tensor_copy(d, s),
                                            inc="dve" if (rp, nlp) == (1, 1) else None)
                        q_tok[0] = tk
                    elif oc < 20:
                        for rp in range(2):
                            for nkl in range(2):
                                r = 2 * (oc - 16) + rp
                                for half in range(2):
                                    tk = P.emit("vector", lambda e,
                                                d=_ap(KH, half * 64 * (2 * T) + nkl * T + r,
                                                      [[2 * T, 64], [8, 256]]),
                                                s=_ap(ps, rp * 64 * 512 + nkl * 256,
                                                      [[512, 64], [1, 256]]):
                                                e.tensor_copy(d, s),
                                                inc="dve" if (rp, nkl, half) == (1, 1, 1)
                                                else None)
                        k_tok[0] = tk
                    else:
                        for rp in range(2):
                            for nkl in range(2):
                                r = 2 * (oc - 20) + rp
                                tk = P.emit("vector", lambda e,
                                            d=_ap(VF2, nkl * 64 * T + r, [[T, 64], [8, 256]]),
                                            s=_ap(ps, rp * 64 * 512 + nkl * 256,
                                                  [[512, 64], [1, 256]]):
                                            e.tensor_copy(d, s),
                                            inc="dve" if (rp, nkl) == (1, 1) else None)
                    return [tk]

                emit_linear(env, wname="d_wqkv", K=D, O=3 * D, rhs_sb=XBT, rhs_pitch=TT,
                            ncols=TT, evac=evac_qkv, rhs_tok=[t_xb], wbuf=W8B)
                xbt_reader[0] = ("pe", P.cnt["pe"])
                t_vf = P.settle("vector", "dve")

                for nkl in range(2):
                    for sc in range(16):
                        PTt = PTA if sc % 2 == 0 else PTB
                        P.wait("tensor", [t_vf, t_vh0, pt_rd[0]] + env.const_tok)
                        t_tr = P.emit("tensor", lambda e, o=PTt[:, 0:64],
                                      i=VF2[nkl * 64:(nkl + 1) * 64, sc * 128:(sc + 1) * 128],
                                      idn=ident[nkl * 64:(nkl + 1) * 64, nkl * 64:(nkl + 1) * 64]:
                                      e.transpose(o, i, idn), inc="pe")
                        P.wait("scalar", [t_tr])
                        tvv = P.emit("scalar", lambda e,
                                     d=VH[:, nkl * 1040 + sc * 65: nkl * 1040 + sc * 65 + 64],
                                     s=PTt[:, 0:64]: e.copy(d, s), inc="act")
                        pt_rd[0] = tvv
                t_vhs = P.settle("scalar", "act")

                for jl in range(4):
                    j = hg * 4 + jl
                    nkl = jl // 2
                    for qt in range(4):
                        PVa, pva_i = env.pool.get(P, hold=True)
                        PVb, pvb_i = env.pool.get(P, hold=True)
                        nkc = 4 * qt + 4
                        t_pv = None
                        prev_ps = [None]
                        for kc in range(nkc):
                            diag = kc >= 4 * qt
                            ts_ab = []
                            sbanks = []
                            for sub in range(2):
                                sbank, sbi = env.pool.get(P)
                                sbanks.append((sbank, sbi))
                                base = sub * 64
                                P.wait("tensor", [q_tok[0], k_tok[0], t_vhs])
                                tsc = P.emit("tensor", lambda e, o=sbank[:, 0:TT],
                                             l=KH[base:base + 64, nkl * T + kc * 128: nkl * T + (kc + 1) * 128],
                                             r=QH[base:base + 64, jl * T + qt * TT: jl * T + (qt + 1) * TT]:
                                             e.matmul(o, l, r, start=True, stop=True,
                                                      skip_group_check=True), inc="pe")
                                ts_ab.append(tsc)
                            tp_ab = []
                            for sub in range(2):
                                sbank, sbi = sbanks[sub]
                                if diag:
                                    m = kc - 4 * qt
                                    P.wait("vector", [ts_ab[sub], t_mk, tp_ab[0] if sub else prev_ps[0]])
                                    tsm = P.emit("vector", lambda e,
                                                 o=SMF[:, sub * TT:(sub + 1) * TT],
                                                 i=sbank[:, 0:TT],
                                                 mk=DMK[:, m * TT:(m + 1) * TT]:
                                                 e.scalar_tensor_tensor(o, i, 0.125, mk,
                                                                        ALU.mult, ALU.add),
                                                 inc="dve")
                                    P.wait("scalar", [tsm, prev_ps[0]])
                                    tpe = P.emit("scalar", lambda e,
                                                 o=PAB[:, sub * TT:(sub + 1) * TT],
                                                 i=SMF[:, sub * TT:(sub + 1) * TT]:
                                                 e.activation(o, i, AF.Exp), inc="act")
                                else:
                                    P.wait("scalar", [ts_ab[sub], prev_ps[0]])
                                    tpe = P.emit("scalar", lambda e,
                                                 o=PAB[:, sub * TT:(sub + 1) * TT],
                                                 i=sbank[:, 0:TT]:
                                                 e.activation(o, i, AF.Exp, scale=0.125),
                                                 inc="act")
                                tp_ab.append(tpe)
                                env.pool.release(sbi, [tpe])
                            t_ps = P.settle("scalar", "act")
                            P.wait("tensor", [t_ps])
                            for sub, PV in ((0, PVa), (1, PVb)):
                                t_pv = P.emit("tensor", lambda e, o=PV[0:65, 0:TT],
                                              l=VH[:, nkl * 1040 + (kc % 16) * 65: nkl * 1040 + (kc % 16) * 65 + 65],
                                              r=PAB[:, sub * TT:(sub + 1) * TT],
                                              s=(kc == 0), p=(kc == nkc - 1):
                                              e.matmul(o, l, r, start=s, stop=p,
                                                       skip_group_check=True),
                                              inc="pe" if sub == 1 else None)
                            prev_ps[0] = t_pv
                        P.wait("scalar", [t_pv] + env.ln_prev)
                        P.emit("scalar", lambda e, i=PVa[64:65, 0:TT]:
                               e.copy(DROW[0:1, 0:TT], i))
                        t_dr = P.emit("scalar", lambda e, i=PVb[64:65, 0:TT]:
                                      e.copy(DROW[0:1, TT:2 * TT], i), inc="act")
                        P.wait("vector", [t_dr, t_l3])
                        P.pause("vector")
                        P.emit("vector", lambda e: e.reciprocal(DROW[0:1, :], DROW[0:1, :]))
                        P.pause("vector")
                        P.emit("vector", lambda e: e.tensor_scalar(DROW[0:1, 0:TT],
                                                                   DROW[0:1, 0:TT],
                                                                   1.0 - LI, None, ALU.mult))
                        P.pause("vector")
                        t_dn = P.emit("vector", lambda e: e.tensor_scalar(
                            DROW[0:1, TT:2 * TT], DROW[0:1, TT:2 * TT], LAMT[0:1, 3:4],
                            None, ALU.mult), inc="dve")
                        P.pause("vector")
                        bcb, bcb_i = env.pool.get(P)
                        P.wait("tensor", [t_dn] + env.const_tok)
                        P.pause("tensor")
                        t_b1 = P.emit("tensor", lambda e, o=bcb[0:64, 0:TT]:
                                      e.matmul(o, env.sb["one1f"][0:1, 0:64],
                                               DROW[0:1, 0:TT], start=True, stop=True,
                                               skip_group_check=True), inc="pe")
                        bcb2, bcb2_i = env.pool.get(P)
                        t_b2 = P.emit("tensor", lambda e, o=bcb2[0:64, 0:TT]:
                                      e.matmul(o, env.sb["one1f"][0:1, 0:64],
                                               DROW[0:1, TT:2 * TT], start=True, stop=True,
                                               skip_group_check=True), inc="pe")
                        P.wait("scalar", [t_b2])
                        P.emit("scalar", lambda e, i=bcb[0:64, 0:TT]:
                               e.copy(BCS2[0:64, 0:TT], i))
                        t_e2 = P.emit("scalar", lambda e, i=bcb2[0:64, 0:TT]:
                                      e.copy(BCS2[0:64, TT:2 * TT], i), inc="act")
                        env.pool.release(bcb_i, [t_e2])
                        env.pool.release(bcb2_i, [t_e2])
                        t_es = P.settle("scalar", "act")
                        P.wait("vector", [t_es, t_pv])
                        P.emit("vector", lambda e, i=PVa[0:64, 0:TT]:
                               e.tensor_tensor(TMPA[0:64, 0:TT], i, BCS2[0:64, 0:TT],
                                               ALU.mult))
                        P.emit("vector", lambda e, i=PVb[0:64, 0:TT]:
                               e.tensor_tensor(TMPA[0:64, TT:2 * TT], i,
                                               BCS2[0:64, TT:2 * TT], ALU.mult))
                        tat = P.emit("vector", lambda e,
                                     d=_ap(ATTN, (j % 2) * 64 * (NCH * T) + (j // 2) * T + qt * TT,
                                           [[NCH * T, 64], [1, TT]]):
                                     e.tensor_tensor(d, TMPA[0:64, 0:TT],
                                                     TMPA[0:64, TT:2 * TT], ALU.add),
                                     inc="dve")
                        attn_toks[0] = tat
                        env.pool.release(pva_i, [tat])
                        env.pool.release(pvb_i, [tat])

        with ExitStack() as es3:
            Y = es3.enter_context(nc.sbuf_tensor("da_y", [128, NCH * T], BF16))
            t_at = P.settle("vector", "dve")
            y_tok = [None]

            def evac_y(oc, w, ps, tmm):
                P.wait("scalar", [tmm])
                tk = P.emit("scalar", lambda e, o=Y[:, oc * T + w * TT: oc * T + w * TT + TT],
                            i=ps[:, 0:TT]: e.copy(o, i), inc="act")
                y_tok[0] = tk
                return [tk]

            emit_linear(env, wname="d_wproj", K=D, O=D, rhs_sb=ATTN, rhs_pitch=T,
                        ncols=TT, nwin=NTT, win_stride=TT, evac=evac_y,
                        rhs_tok=[t_at, attn_toks[0]], wbuf=W8B)

            t_ys = P.settle("scalar", "act")
            y2_tok = [None]
            for h in range(2):
                for cb in range(8):
                    for jb in range(8):
                        PTt = PTA if jb % 2 == 0 else PTB
                        P.wait("tensor", [t_ys, y2_tok[0] if (h or cb or jb) else None,
                                          pt_rd[0]] + env.const_tok)
                        t_tr = P.emit("tensor", lambda e, o=PTt[:, 0:128],
                                      i=Y[:, cb * T + h * 1024 + jb * 128: cb * T + h * 1024 + (jb + 1) * 128]:
                                      e.transpose(o, i, ident[:, :]), inc="pe")
                        P.wait("vector", [t_tr])
                        tt2 = P.emit("vector", lambda e,
                                     d=_ap(ATTN, jb * T + cb * 256 + h, [[NCH * T, 128], [2, 128]]),
                                     s=PTt[:, 0:128]: e.tensor_copy(d, s), inc="dve")
                        y2_tok[0] = tt2
                        pt_rd[0] = tt2

            t_y2 = P.settle("vector", "dve")
            s2_tok = [None]

            def evac_a2(oc, w, ps, tmm):
                sl = stream_slice(env, oc, w * TT, TT)
                P.wait("vector", [tmm, t_y2] + env.stream_tok)
                tk = P.emit("vector", lambda e, sl=sl, i=ps[:, 0:TT]:
                            e.tensor_tensor(sl, sl, i, ALU.add), inc="dve")
                s2_tok[0] = tk
                env.stream_tok = [tk]
                return [tk]

            emit_linear(env, wname="d_wproj", K=D, O=D, rhs_sb=ATTN, rhs_pitch=T,
                        ncols=TT, nwin=NTT, win_stride=TT, evac=evac_a2,
                        rhs_tok=[t_y2], wbuf=W8B)

        for q in range(NTT):
            ln_toks = emit_postln(env, q * TT, TT, ABF, ASQ)
            env.stream_tok = ln_toks


DBG = False
TRUNC = 0


def declare_l1ffn(nc, env):
    if DBG:
        env.dram["dbg_cw"] = nc.dram_tensor("dbg_cw", [8, T], mybir.dt.bfloat16, kind="ExternalOutput")
        env.dram["dbg_gl"] = nc.dram_tensor("dbg_gl", [128, 512], F32, kind="ExternalOutput")
        env.dram["dbg_xb"] = nc.dram_tensor("dbg_xb", [128, NCH * T], mybir.dt.bfloat16, kind="ExternalOutput")
        env.dram["dbg_gw"] = nc.dram_tensor("dbg_gw", [128, 64], mybir.dt.bfloat16, kind="ExternalOutput")
        env.dram["dbg_gl2"] = nc.dram_tensor("dbg_gl2", [128, 512], F32, kind="ExternalOutput")
        env.dram["dbg_cwt"] = nc.dram_tensor("dbg_cwt", [128, 128], mybir.dt.bfloat16,
                                             kind="ExternalOutput")
        for i in range(2, 6):
            env.dram[f"dbg_xb{i}"] = nc.dram_tensor(f"dbg_xb{i}", [128, NCH * T],
                                                    mybir.dt.bfloat16, kind="ExternalOutput")
        env.dram["dbg_st"] = nc.dram_tensor("dbg_st", [128, NCH * T], F32, kind="ExternalOutput")
        env.dram["dbg_he"] = nc.dram_tensor("dbg_he", [128, 16 * T], F32, kind="ExternalOutput")
    env.dram["m_gw"] = nc.dram_tensor("m_gw", [D, 8], F32, kind="ExternalInput")
    env.dram["m_w1"] = nc.dram_tensor("m_w1", [8 * D, 2048], F32, kind="ExternalInput")
    env.dram["m_w2"] = nc.dram_tensor("m_w2", [8 * 2048, D], F32, kind="ExternalInput")
    env.dram["m_b1"] = nc.dram_tensor("m_b1", [128, 128], F32, kind="ExternalInput")
    env.dram["m_b2"] = nc.dram_tensor("m_b2", [8, D], F32, kind="ExternalInput")
    env.dram["m_sel"] = nc.dram_tensor("m_sel", [8, D], F32, kind="ExternalInput")


def emit_l1ffn(env):
    nc, P = env.nc, env.P
    with ExitStack() as es:
        XB = es.enter_context(nc.sbuf_tensor("mx", [128, NCH * T], BF16))
        HE = es.enter_context(nc.sbuf_tensor("mhe", [128, 16 * T], BF16))
        GW = es.enter_context(nc.sbuf_tensor("mgw", [128, 64], BF16))
        SEL8 = es.enter_context(nc.sbuf_tensor("msel", [8, D], BF16))
        B2M = es.enter_context(nc.sbuf_tensor("mb2", [8, D], BF16))
        B1R = es.enter_context(nc.sbuf_tensor("mb1", [128, 128], F32))
        CW8 = es.enter_context(nc.sbuf_tensor("mcw8", [8, T], BF16))
        GL = es.enter_context(nc.sbuf_tensor("mgl", [128, 512], F32))
        CWT16 = es.enter_context(nc.sbuf_tensor("mcwt16", [128, 128], BF16))
        CWB = es.enter_context(nc.sbuf_tensor("mcwb", [128, 4 * TT], BF16))
        TMP = es.enter_context(nc.sbuf_tensor("mtmp", [128, TT], F32))
        W1B = es.enter_context(nc.sbuf_tensor("mw1b", [128, 2 * 8 * 128], BF16))
        W2B = es.enter_context(nc.sbuf_tensor("mw2b", [128, 2 * 16 * 128], BF16))
        ABF = es.enter_context(nc.sbuf_tensor("mabf", [128, NCH * 128], BF16))
        ASQ = es.enter_context(nc.sbuf_tensor("masq", [128, NCH * 128], BF16))
        PT = es.enter_context(nc.psum_tensor("mpt", [128, 128], BF16))

        gw = env.dram["m_gw"]
        t_gw = P.dma("gpsimd", _ap(GW, 0, [[64, 128], [8, 8], [1, 8]]),
                     _ap(gw, 0, [[8, 128], [128 * 8, 8], [1, 8]]), "qw")
        t_sel = P.dma("gpsimd", SEL8[:, :], env.dram["m_sel"].ap()[:, :], "qw")
        t_b2 = P.dma("gpsimd", B2M[:, :], env.dram["m_b2"].ap()[:, :], "qw")
        t_b1 = P.dma("sync", B1R[:, :], env.dram["m_b1"].ap()[:, :], "qx")

        if DBG:
            P.wait("sync", env.stream_tok)
            t_dst = P.dma("sync", env.dram["dbg_st"].ap()[:, :], env.sb["stream"][:, :], "qx")
            env.stream_tok = env.stream_tok + [t_dst]
        P.wait("vector", env.stream_tok)
        t_xb = P.emit("vector", lambda e: e.tensor_copy(XB[:, :], env.sb["stream"][:, :]),
                      inc="dve")
        if DBG:
            P.wait("sync", [t_xb])
            P.dma("sync", env.dram["dbg_xb"].ap()[:, :], XB[:, :], "qx")
        P.wait("vector", [t_xb])
        t_dbl = None
        for fc in range(NCH):
            sl = env.sb["stream"][:, fc * T:(fc + 1) * T]
            t_dbl = P.emit("vector", lambda e, sl=sl: e.tensor_scalar(sl, sl, 2.0, None, ALU.mult),
                           inc="dve" if fc == NCH - 1 else None)
        env.stream_tok = [t_dbl]

        t_glz = P.emit("vector", lambda e: e.memset(GL[:, :], 0.0), inc="dve")
        ps, bi = env.pool.get(P)
        P.wait("tensor", [t_xb, t_gw, t_sel, t_b2])
        tg = None
        for tc in range(16):
            for kc in range(NCH):
                tg = P.emit("tensor", lambda e, o=ps[:, tc * 16: tc * 16 + 8],
                            l=XB[:, kc * T + tc * 128: kc * T + (tc + 1) * 128],
                            r=GW[:, kc * 8:(kc + 1) * 8], s=(kc == 0),
                            p=(kc == NCH - 1):
                            e.matmul(o, l, r, start=s, stop=p, skip_group_check=True),
                            inc="pe" if (tc == 15 and kc == NCH - 1) else None)
        P.wait("scalar", [tg, t_glz])
        for tc in range(16):
            t_lg = P.emit("scalar", lambda e, d=GL[:, tc * 8:(tc + 1) * 8],
                          i=ps[:, tc * 16: tc * 16 + 8]: e.copy(d, i),
                          inc="act" if tc == 15 else None)
        env.pool.release(bi, [t_lg])

        if DBG:
            P.wait("sync", [t_lg])
            P.dma("sync", env.dram["dbg_xb2"].ap()[:, :], XB[:, :], "qx")
            P.dma("sync", env.dram["dbg_gl2"].ap()[:, :], GL[:, :], "qx")

        if TRUNC == 1:
            return

        def bc16(t, col0):
            return _ap(t, col0, [[512, 128], [1, 16], [0, 8]])

        def g3(col0):
            return _ap(GL, col0, [[512, 128], [8, 16], [1, 8]])

        g_ = GL[:, 0:128]
        e_ = GL[:, 128:256]
        w_ = GL[:, 256:384]
        M1 = GL[:, 384:400]
        SM = GL[:, 400:416]
        MG = GL[:, 416:432]
        M2 = GL[:, 432:448]
        DN = GL[:, 448:464]
        RD = GL[:, 464:480]
        P.wait("vector", [t_lg])
        P.emit("vector", lambda e: e.tensor_reduce(M1, _ap(GL, 0, [[GL.shape[1], 128], [8, 16], [1, 8]]),
                                                   AX.X, ALU.max))
        P.pause("vector")
        P.emit("vector", lambda e: e.tensor_scalar(M1, M1, -1.0, None, ALU.mult))
        P.pause("vector")
        t_sub = P.emit("vector", lambda e: e.tensor_tensor(g3(128), g3(0), bc16(GL, 384),
                                                           ALU.add), inc="dve")
        P.pause("vector")
        P.wait("scalar", [t_sub])
        P.pause("scalar")
        t_exp = P.emit("scalar", lambda e: e.activation(e_, e_, AF.Exp), inc="act")
        P.wait("vector", [t_exp])
        P.emit("vector", lambda e: e.tensor_reduce(SM, _ap(GL, 128, [[GL.shape[1], 128], [8, 16], [1, 8]]),
                                                   AX.X, ALU.add))
        P.pause("vector")
        P.emit("vector", lambda e: e.reciprocal(SM, SM))
        P.pause("vector")
        P.emit("vector", lambda e: e.tensor_tensor(g3(256), g3(128), bc16(GL, 400), ALU.mult))
        P.pause("vector")
        P.emit("vector", lambda e: e.tensor_reduce(MG, _ap(GL, 256, [[GL.shape[1], 128], [8, 16], [1, 8]]),
                                                   AX.X, ALU.max))
        P.pause("vector")
        P.emit("vector", lambda e: e.tensor_tensor(g3(128), g3(256), bc16(GL, 416), ALU.is_equal))
        P.pause("vector")
        P.emit("vector", lambda e: e.scalar_tensor_tensor(e_, e_, -1e9, w_, ALU.mult, ALU.add))
        P.pause("vector")
        P.emit("vector", lambda e: e.tensor_reduce(M2, _ap(GL, 128, [[GL.shape[1], 128], [8, 16], [1, 8]]),
                                                   AX.X, ALU.max))
        P.pause("vector")
        P.emit("vector", lambda e: e.tensor_tensor(g3(128), g3(256), bc16(GL, 432), ALU.is_ge))
        P.pause("vector")
        P.emit("vector", lambda e: e.tensor_tensor(w_, w_, e_, ALU.mult))
        P.pause("vector")
        P.emit("vector", lambda e: e.tensor_tensor(DN, MG, M2, ALU.add))
        P.pause("vector")
        P.emit("vector", lambda e: e.tensor_scalar(DN, DN, 1e-8, None, ALU.add))
        P.pause("vector")
        P.emit("vector", lambda e: e.reciprocal(RD, DN))
        P.pause("vector")
        t_cw = P.emit("vector", lambda e: e.tensor_tensor(
            _ap(CWT16, 0, [[128, 128], [8, 16], [1, 8]]), g3(256), bc16(GL, 464),
            ALU.mult), inc="dve")
        P.pause("vector")
        if DBG:
            P.wait("sync", [t_cw])
            P.dma("sync", env.dram["dbg_gl"].ap()[:, :], GL[:, :], "qx")
            P.dma("sync", env.dram["dbg_cwt"].ap()[:, :], CWT16[:, :], "qx")
            P.dma("sync", env.dram["dbg_gw"].ap()[:, :], GW[:, :], "qx")
        if TRUNC == 2:
            return
        if DBG:
            P.wait("sync", [t_cw])
            P.dma("sync", env.dram["dbg_xb3"].ap()[:, :], XB[:, :], "qx")
        t_pt_reader = None
        t_c8 = None
        for tc in range(16):
            P.wait("tensor", [t_cw, t_pt_reader] + env.const_tok)
            if tc == 0:
                P.pause("tensor")
            t_tr = P.emit("tensor", lambda e, i=CWT16[:, tc * 8:(tc + 1) * 8]:
                          e.transpose(PT[0:8, 0:128], i, env.sb["ident"][:, :]), inc="pe")
            P.wait("scalar", [t_tr])
            t_c8 = P.emit("scalar", lambda e, d=CW8[:, tc * 128:(tc + 1) * 128]:
                          e.copy(d, PT[0:8, 0:128]), inc="act")
            t_pt_reader = t_c8

        if TRUNC == 3:
            return
        if DBG:
            P.wait("tensor", P.queue_flush_tokens("qx"))
        if DBG:
            P.wait("sync", [t_c8])
            P.dma("sync", env.dram["dbg_xb4"].ap()[:, :], XB[:, :], "qx")
        for oc in range(NCH):
            for q in range(NTT):
                ps, bi = env.pool.get(P)
                P.wait("tensor", [t_b2, t_c8])
                if TRUNC != 42:
                    tb = P.emit("tensor", lambda e, o=ps[:, 0:TT],
                                l=B2M[:, oc * 128:(oc + 1) * 128],
                                r=CW8[:, q * TT:(q + 1) * TT]:
                                e.matmul(o, l, r, start=True, stop=True), inc="pe")
                else:
                    tb = P.emit("tensor", lambda e: e.nop(), inc="pe")
                sl = stream_slice(env, oc, q * TT, TT)
                P.wait("vector", [tb] + env.stream_tok)
                if TRUNC != 41 and TRUNC != 42:
                    tk = P.emit("vector", lambda e, sl=sl, i=ps[:, 0:TT]:
                                e.tensor_tensor(sl, sl, i, ALU.add), inc="dve")
                else:
                    tk = P.emit("vector", lambda e: e.nop(), inc="dve")
                env.stream_tok = [tk]
                env.pool.release(bi, [tk])
        if TRUNC in (41, 42):
            return

        if DBG:
            P.wait("sync", env.stream_tok)
            P.dma("sync", env.dram["dbg_xb5"].ap()[:, :], XB[:, :], "qx")
        if TRUNC == 4:
            return
        cwb_reader = None
        for ex in range(8):
            s1_last = [None]

            def evac_s1(oc, w, ps, tmm, ex=ex, s1_last=s1_last):
                P.wait("scalar", [tmm, t_b1])
                tk = P.emit("scalar", lambda e, o=HE[:, oc * T + w * TT: oc * T + w * TT + TT],
                            i=ps[:, 0:TT], b=B1R[:, ex * 16 + oc: ex * 16 + oc + 1]:
                            e.activation(o, i, AF.Relu, bias=b), inc="act")
                s1_last[0] = tk
                return [tk]

            emit_linear(env, wname="m_w1", woff=ex * D * 2048, K=D, O=2048, rhs_sb=XB,
                        rhs_pitch=T, ncols=TT, nwin=NTT, win_stride=TT, evac=evac_s1,
                        rhs_tok=[t_xb], wbuf=W1B)

            cwb_toks = []
            for q in range(NTT):
                ps, bi = env.pool.get(P)
                P.wait("tensor", [t_sel, t_c8])
                tb = P.emit("tensor", lambda e, o=ps[:, 0:TT],
                            l=SEL8[:, ex * 128:(ex + 1) * 128],
                            r=CW8[:, q * TT:(q + 1) * TT]:
                            e.matmul(o, l, r, start=True, stop=True), inc="pe")
                P.wait("scalar", [tb, cwb_reader])
                tcb = P.emit("scalar", lambda e, o=CWB[:, q * TT:(q + 1) * TT], i=ps[:, 0:TT]:
                             e.copy(o, i), inc="act")
                cwb_toks.append(tcb)
                env.pool.release(bi, [tcb])

            s2_last = [None]

            def evac_s2(oc, w, ps, tmm, cwb_toks=cwb_toks, s2_last=s2_last):
                tm = TMP[:, 0:TT]
                P.wait("vector", [tmm, cwb_toks[w]] + env.stream_tok)
                P.emit("vector", lambda e, o=tm, i=ps[:, 0:TT],
                       c=CWB[:, w * TT:(w + 1) * TT]: e.tensor_tensor(o, i, c, ALU.mult))
                sl = stream_slice(env, oc, w * TT, TT)
                tk = P.emit("vector", lambda e, sl=sl, t=tm: e.tensor_tensor(sl, sl, t, ALU.add),
                            inc="dve")
                s2_last[0] = tk
                env.stream_tok = [tk]
                return [tk]

            if DBG and ex == 0:
                P.wait("gpsimd", [s1_last[0]])
                P.dma("gpsimd", env.dram["dbg_he"].ap()[:, :], HE[:, :], "qw")
            emit_linear(env, wname="m_w2", woff=ex * 2048 * D, K=2048, O=D, rhs_sb=HE,
                        rhs_pitch=T, ncols=TT, nwin=NTT, win_stride=TT, evac=evac_s2,
                        rhs_tok=[s1_last[0]], wbuf=W2B)
            cwb_reader = s2_last[0]

        if TRUNC == 5:
            return
        for q in range(16):
            ln_toks = emit_postln(env, q * 128, 128, ABF, ASQ)
            env.stream_tok = ln_toks


DECL = {"l0attn": declare_l0attn, "l0ffn": declare_l0ffn,
        "l1attn": declare_l1attn, "l1ffn": declare_l1ffn}
EMIT = {"l0attn": emit_l0attn, "l0ffn": emit_l0ffn,
        "l1attn": emit_l1attn, "l1ffn": emit_l1ffn}


RACEDET = True


def build_program(blocks):
    nc = bass.Bass("TRN2", target_bir_lowering=False, detect_race_conditions=RACEDET)
    P = Pl(nc)
    env = Env(nc, P)
    declare_common(nc, env)
    for b in blocks:
        DECL[b](nc, env)
    env.pool = BankPool(nc, 6)
    emit_prelude(env)
    for i, b in enumerate(blocks):
        if i:
            env.P.phase_barrier()
        EMIT[b](env)
    emit_epilogue(env)
    P.finalize()
    return nc



def _prep_weights(inputs, blocks):
    w = {}
    w["cid"] = np.eye(128, dtype=np.float32)
    w["cmask"] = np.triu(np.ones((128, 128), dtype=np.float32))
    if "l0attn" in blocks:
        w["a_wqkvT"] = np.ascontiguousarray(np.asarray(inputs["L0_attn_Wqkv"], dtype=np.float32).T)
        w["a_woutT"] = np.ascontiguousarray(np.asarray(inputs["L0_attn_Wout"], dtype=np.float32).T)
        w["a_bqkv"] = _colmajor128(np.asarray(inputs["L0_attn_bqkv"], dtype=np.float32))
        w["a_bout"] = _colmajor128(np.asarray(inputs["L0_attn_bout"], dtype=np.float32))
        w["a_gamma"] = _colmajor128(np.asarray(inputs["L0_attn_gamma"], dtype=np.float32))
        w["a_beta"] = _colmajor128(np.asarray(inputs["L0_attn_beta"], dtype=np.float32))
    if "l0ffn" in blocks:
        w["f_w1T"] = np.ascontiguousarray(np.asarray(inputs["L0_ffn_w1_w"]).T)
        w["f_w2T"] = np.ascontiguousarray(np.asarray(inputs["L0_ffn_w2_w"]).T)
        w["f_h1T"] = np.ascontiguousarray(np.asarray(inputs["L0_ffn_h1_w"]).T)
        w["f_h2T"] = np.ascontiguousarray(np.asarray(inputs["L0_ffn_h2_w"]).T)
        w["f_w3T"] = np.ascontiguousarray(np.asarray(inputs["L0_ffn_w3_w"]).T)
        w["f_s1"] = _colmajor128(np.asarray(inputs["L0_ffn_w1_s"]))
        w["f_s2"] = _colmajor128(np.asarray(inputs["L0_ffn_w2_s"]))
        w["f_sh1"] = _colmajor128(np.asarray(inputs["L0_ffn_h1_s"]))
        w["f_sh2"] = _colmajor128(np.asarray(inputs["L0_ffn_h2_s"]))
        w["f_s3"] = _colmajor128(np.asarray(inputs["L0_ffn_w3_s"]))
    if "l1attn" in blocks:
        w["d_wqkv"] = np.ascontiguousarray(np.asarray(inputs["L1_attn_Wqkv"], dtype=np.float32))
        w["d_wproj"] = np.ascontiguousarray(np.asarray(inputs["L1_attn_Wproj"], dtype=np.float32))
        w["d_lam"] = np.stack([np.asarray(inputs["L1_attn_lambda_q1"], dtype=np.float32),
                               np.asarray(inputs["L1_attn_lambda_k1"], dtype=np.float32),
                               np.asarray(inputs["L1_attn_lambda_q2"], dtype=np.float32),
                               np.asarray(inputs["L1_attn_lambda_k2"], dtype=np.float32)])
        dm = np.zeros((128, 4, 512), dtype=np.float32)
        for m in range(4):
            for k_ in range(128):
                dm[k_, m, :] = np.where(np.arange(512) >= k_ + 128 * m, 0.0, -1e9)
        w["d_dmask"] = np.ascontiguousarray(dm.reshape(128, 2048))
    if "l1ffn" in blocks:
        w["m_gw"] = np.ascontiguousarray(np.asarray(inputs["L1_ffn_gate_W"], dtype=np.float32))
        w["m_w1"] = np.ascontiguousarray(np.asarray(inputs["L1_ffn_W1"], dtype=np.float32).reshape(8 * D, 2048))
        w["m_w2"] = np.ascontiguousarray(np.asarray(inputs["L1_ffn_W2"], dtype=np.float32).reshape(8 * 2048, D))
        b1 = np.asarray(inputs["L1_ffn_b1"], dtype=np.float32)
        w["m_b1"] = np.ascontiguousarray(b1.reshape(8, 16, 128).transpose(2, 0, 1).reshape(128, 128))
        w["m_b2"] = np.ascontiguousarray(np.asarray(inputs["L1_ffn_b2"], dtype=np.float32))
        w["m_sel"] = np.kron(np.eye(8, dtype=np.float32), np.ones((1, 128), dtype=np.float32))
    return w


def _colmajor128(v):
    n = v.shape[0] // 128
    return np.ascontiguousarray(v.reshape(n, 128).T)


def run_blocks(x_full, inputs, blocks):
    nc = build_program(blocks)
    w = _prep_weights(inputs, blocks)
    in_maps = []
    for b in range(8):
        m = dict(w)
        m["x_in"] = np.ascontiguousarray(x_full[b].T)
        in_maps.append(m)
    res = run_bass_kernel_spmd(nc, in_maps, core_ids=list(range(8)))
    return np.stack([np.asarray(r["y_out"]).T for r in res.results], axis=0)


def kernel(**inputs):
    x = np.asarray(inputs["L0_attn_x"], dtype=np.float32)
    for grp in GROUPS:
        x = run_blocks(x, inputs, grp)
    return x
